# version 08 (final)
# baseline (speedup 1.0000x reference)
import contextlib
import math
import os

import numpy as np
import ml_dtypes

import concourse.bass as bass
import concourse.mybir as mybir
from concourse.bass_utils import run_bass_kernel_spmd

F32 = mybir.dt.float32
BF16 = mybir.dt.bfloat16
AF = mybir.ActivationFunctionType
ALU = mybir.AluOpType
AX = mybir.AxisListType

B, N, E = 4, 2048, 2048
H, D = 16, 128
HID = 5632
EC = 16
HC = 44
LN_EPS = 1e-5
NORM_EPS = 1e-5
ROPE_BASE = 10000.0
PI = math.pi
CLAMP = 3.1415925

TAUS = (0, 1)
SLOTS = {0: list(range(0, 4)) + list(range(8, 12)),
         1: list(range(0, 8)) + list(range(8, 16))}
LADDER = (128.0, 64.0, 32.0, 16.0, 8.0, 4.0, 2.0, 1.0)


def slot_tpos(s):
    return s * 128 if s < 8 else 1024 + (s - 8) * 128


class Builder:
    def __init__(self, debug=(), stop=None):
        self.nc = bass.Bass("TRN2", target_bir_lowering=False, debug=False,
                            detect_race_conditions=False)
        self.stop = stop
        self.dbg_names = set(debug)
        self.cnt = {}
        self.dbg = {}

    def inc(self, instr, sem, n=1):
        instr.then_inc(sem, n)
        self.cnt[sem.name] = self.cnt.get(sem.name, 0) + n
        return self.cnt[sem.name]

    def val(self, sem):
        return self.cnt.get(sem.name, 0)

    def fence(self, name):
        P, blk, sm = self, self.blk, self.sm
        fs = sm["fS"]
        vals = [(sm[k], P.val(sm[k])) for k in ("sV", "sS", "sT", "sG", "sC", "dS", "dA", "dO")]

        def f_sync(sync):
            for s_, v_ in vals:
                if v_:
                    sync.wait_ge(s_, v_)
            sync.sem_inc(fs, 1)
        blk.sync(f_sync)
        self.cnt[fs.name] = self.cnt.get(fs.name, 0) + 1
        fv = self.cnt[fs.name]
        for eng in ("vector", "scalar", "tensor", "gpsimd"):
            getattr(blk, eng)(lambda e, fv=fv: e.wait_ge(fs, fv))

    def build(self):
        nc = self.nc
        ap = lambda t: t.ap()
        self.x_own = ap(nc.dram_tensor("x_own", [1024, E], F32, kind="ExternalInput"))
        self.x_extra = ap(nc.dram_tensor("x_extra", [1024, E], F32, kind="ExternalInput"))
        self.w_qkv = ap(nc.dram_tensor("w_qkv", [E, 3 * E], F32, kind="ExternalInput"))
        self.w_out = ap(nc.dram_tensor("w_out", [E, E], F32, kind="ExternalInput"))
        self.w13T = ap(nc.dram_tensor("w13T", [E, 2 * HID], F32, kind="ExternalInput"))
        self.w2T = ap(nc.dram_tensor("w2T", [HID, E], F32, kind="ExternalInput"))
        self.ln_g = ap(nc.dram_tensor("ln_g", [D], F32, kind="ExternalInput"))
        self.ln_b = ap(nc.dram_tensor("ln_b", [D], F32, kind="ExternalInput"))
        self.norm_w = ap(nc.dram_tensor("norm_w", [HID], F32, kind="ExternalInput"))
        self.cos_tab = ap(nc.dram_tensor("cos_tab", [128, 16 * 64], F32, kind="ExternalInput"))
        self.sin_tab = ap(nc.dram_tensor("sin_tab", [128, 16 * 64], F32, kind="ExternalInput"))
        self.masks = ap(nc.dram_tensor("masks", [2, 16, 128, 512], BF16, kind="ExternalInput"))
        self.eye_bf = ap(nc.dram_tensor("eye_bf", [128, 128], BF16, kind="ExternalInput"))
        self.eye_f32 = ap(nc.dram_tensor("eye_f32", [128, 128], F32, kind="ExternalInput"))
        self.eh16 = ap(nc.dram_tensor("eh16", [128, 256], BF16, kind="ExternalInput"))
        self.out = ap(nc.dram_tensor("out", [1024, E], F32, kind="ExternalOutput"))

        self.q_scr = ap(nc.dram_tensor("q_scr", [H, 128, 1024], BF16, kind="Internal"))
        self.s_scr = ap(nc.dram_tensor("s_scr", [128, 8 * E], F32, kind="Internal"))
        self.w2b_scr = ap(nc.dram_tensor("w2b_scr", [HID, E], BF16, kind="Internal"))
        self.v_scr = ap(nc.dram_tensor("v_scr", [16, 128, E], BF16, kind="Internal"))
        self.xq_scr = ap(nc.dram_tensor("xq_scr", [EC, 128, 1024], BF16, kind="Internal"))
        self.g_scr = ap(nc.dram_tensor("g_scr", [8, 128, HID], F32, kind="Internal"))
        self.iq_scr = ap(nc.dram_tensor("iq_scr", [HC, 128, 1024], BF16, kind="Internal"))

        for name, shape, dt in [
            ("xn_fm", [128, EC * 2048], BF16),
            ("xq", [EC, 128, 1024], BF16),
            ("kfm", [128, H * 2048], BF16), ("qfm", [H, 128, 1024], BF16),
            ("vtm", [16, 128, E], BF16), ("s_tm", [128, 8 * E], F32),
            ("gated", [8, 128, HID], F32), ("iq", [HC, 128, 1024], BF16),
            ("lw0", [128, 2048], BF16), ("red0", [128, 32], F32),
            ("scales", [128, 24], F32),
            ("ffn", [8, 128, 2048], F32),
        ]:
            if name in self.dbg_names:
                self.dbg[name] = ap(nc.dram_tensor("dbg_" + name, shape, dt,
                                                   kind="ExternalOutput"))

        es = contextlib.ExitStack()
        with es:
            self.es = es
            sbuf = lambda n, s, d=F32: es.enter_context(nc.sbuf_tensor(n, s, d))
            self.sbuf = sbuf
            es_xn = contextlib.ExitStack()
            self.xn_fm = es_xn.enter_context(nc.sbuf_tensor("xn_fm", [128, EC * 2048], BF16, side="right"))
            self.cos_t = es_xn.enter_context(nc.sbuf_tensor("cos_t", [128, 16 * 64], F32, side="right"))
            self.sin_t = es_xn.enter_context(nc.sbuf_tensor("sin_t", [128, 16 * 64], F32, side="right"))
            self.lng_c = sbuf("lng_c", [128, 1], F32)
            self.lnb_c = sbuf("lnb_c", [128, 1], F32)
            self.c128 = sbuf("c128", [128, 1], F32)
            self.c1285 = sbuf("c1285", [128, 1], F32)
            self.c0 = sbuf("c0", [128, 1], F32)
            self.eyeb = sbuf("eyeb", [128, 128], BF16)
            self.eyef = sbuf("eyef", [128, 128], F32)
            self.ehb = sbuf("ehb", [128, 256], BF16)
            self.rs_all = sbuf("rs_all", [128, 8], F32)
            self.r2_all = sbuf("r2_all", [128, 8], F32)
            self.ssq = sbuf("ssq", [128, 8], F32)
            self.red = sbuf("red", [128, 32], F32)

            sems = {}
            for nm in ("dS", "dA", "dO", "sV", "sS", "sT", "sG", "sC", "fS"):
                sems[nm] = es.enter_context(nc.semaphore(nm))
            self.sm = sems

            with nc.Block() as blk:
                self.blk = blk
                with contextlib.ExitStack() as es_p0:
                    self.es_p0 = es_p0
                    with nc.named_scope("p0"):
                        self.p0_setup()
                self.fence("p0")
                with nc.named_scope("p1"):
                    self.p1_ln_quant()
                self.fence("p1")
                if self.stop == "p1":
                    self.finalize(); return nc
                es_k = contextlib.ExitStack()
                self.k_fm = es_k.enter_context(nc.sbuf_tensor("k_fm", [128, H * 2048], BF16))
                with nc.named_scope("p2"):
                    self.p2_qkv()
                self.fence("p2")
                es_xn.close()
                if self.stop == "p2":
                    es_k.close()
                    self.finalize(); return nc
                es_s = contextlib.ExitStack()
                self.es_s = es_s
                self.s_tm = es_s.enter_context(nc.sbuf_tensor("s_tm", [128, 8 * E], F32, side="right"))
                for tau in TAUS:
                    with nc.named_scope(f"p34_{tau}"):
                        self.p34_tau(tau)
                    self.fence(f"p34_{tau}")
                es_k.close()
                if self.stop == "p4":
                    es_s.close()
                    self.finalize(); return nc
                P = self

                def f_ssp(scalar):
                    P.inc(scalar.dma_start(out=self.s_scr, in_=self.s_tm[:]),
                          self.sm["dA"], 16)
                self.blk.scalar(f_ssp)
                with nc.named_scope("p5"):
                    self.p5_ffn()
                self.finalize()
        return nc

    def p0_setup(self):
        P, blk, sm, nc = self, self.blk, self.sm, self.nc
        dS, sV, sS = sm["dS"], sm["sV"], sm["sS"]

        def f_sync(sync):
            P.inc(sync.dma_start(out=self.lng_c[:], in_=self.ln_g.rearrange("(p o) -> p o", o=1)), dS, 16)
            P.inc(sync.dma_start(out=self.lnb_c[:], in_=self.ln_b.rearrange("(p o) -> p o", o=1)), dS, 16)
            P.inc(sync.dma_start(out=self.eyeb[:], in_=self.eye_bf[:, :]), dS, 16)
            P.inc(sync.dma_start(out=self.eyef[:], in_=self.eye_f32[:, :]), dS, 16)
            P.inc(sync.dma_start(out=self.ehb[:], in_=self.eh16[:, :]), dS, 16)
            P.inc(sync.dma_start(out=self.cos_t[:], in_=self.cos_tab[:, :]), dS, 16)
            P.inc(sync.dma_start(out=self.sin_t[:], in_=self.sin_tab[:, :]), dS, 16)
        blk.sync(f_sync)

        def f_vec(vector):
            vector.memset(self.c128[:], -128.0)
            vector.memset(self.c1285[:], 128.5)
            P.inc(vector.memset(self.c0[:], 0.0), sV, 1)
        blk.vector(f_vec)
        self.rope_ok = 0

    def ladder(self, wbuf, w0buf, cbuf, width, out_bf, final_add):
        P, blk, sm = self, self.blk, self.sm
        sV, sS = sm["sV"], sm["sS"]

        def f_v0(vector):
            P.inc(vector.tensor_copy(w0buf[:, 0:width], wbuf[:, 0:width]), sV, 1)
        blk.vector(f_v0)
        prev = P.val(sV)
        for v in LADDER:
            def f_s(scalar, prev=prev):
                scalar.wait_ge(sV, prev)
                P.inc(scalar.activation(cbuf[:, 0:width], wbuf[:, 0:width],
                                        AF.Sign, bias=self.c128[:]), sS, 1)
            blk.scalar(f_s)
            sok = P.val(sS)

            def f_v(vector, v=v, sok=sok):
                vector.wait_ge(sS, sok)
                P.inc(vector.scalar_tensor_tensor(wbuf[:, 0:width], cbuf[:, 0:width],
                                                  -v / 2.0, wbuf[:, 0:width],
                                                  ALU.mult, ALU.add), sV, 1)
            blk.vector(f_v)
            prev = P.val(sV)

        def f_vf(vector):
            vector.scalar_tensor_tensor(wbuf[:, 0:width], wbuf[:, 0:width], -1.0,
                                        w0buf[:, 0:width], ALU.mult, ALU.add)
            P.inc(vector.tensor_scalar(out_bf[:, 0:width], wbuf[:, 0:width],
                                       final_add - 0.5, None, ALU.add), sV, 1)
        blk.vector(f_vf)
        return P.val(sV)

    def transpose_batch(self, tiles, ps_t, dst_emit, wait_list):
        P, blk, sm = self, self.blk, self.sm
        sT, sS = sm["sT"], sm["sS"]
        evac_prev = self._pt_evac

        def f_t(tensor):
            for s_, v_ in wait_list:
                tensor.wait_ge(s_, v_)
            if evac_prev:
                tensor.wait_ge(sS, evac_prev)
            for j, src in enumerate(tiles):
                ins = tensor.transpose(ps_t[:, j * 128:(j + 1) * 128], src, self.eyeb[:])
            P.inc(ins, sT, 1)
        blk.tensor(f_t)
        tdone = P.val(sT)

        def f_se(scalar):
            scalar.wait_ge(sT, tdone)
            P.inc(dst_emit(scalar, ps_t), sS, 1)
        blk.scalar(f_se)
        self._pt_evac = P.val(sS)
        return self._pt_evac

    def p1_ln_quant(self):
        P, blk, sm, nc = self, self.blk, self.sm, self.nc
        dS, dA, sV, sS, sT = (sm[k] for k in ("dS", "dA", "sV", "sS", "sT"))
        with contextlib.ExitStack() as ph:
            ps_t = ph.enter_context(nc.psum_tensor("ps_t1", [128, 1024], BF16))
            xa = ph.enter_context(nc.sbuf_tensor("xa", [128, 2048], F32))
            xb2 = ph.enter_context(nc.sbuf_tensor("xb2", [128, 2048], F32))
            wrk = ph.enter_context(nc.sbuf_tensor("wrk", [128, 2048], F32))
            w0b = ph.enter_context(nc.sbuf_tensor("w0b", [128, 2048], F32))
            cbf = ph.enter_context(nc.sbuf_tensor("cbf", [128, 2048], F32))
            bft = ph.enter_context(nc.sbuf_tensor("bft", [128, 2048], BF16))
            xqst = ph.enter_context(nc.sbuf_tensor("xqst", [128, 2048], BF16))
            sdump = ph.enter_context(nc.sbuf_tensor("sdump", [128, 2048], F32))
            red = self.red
            self._pt_evac = 0

            cons = [0, 0]
            spill_prev = 0
            p1n = int(os.environ.get("P1N", "16"))
            for bi16 in range(p1n):
                own = bi16 < 8
                bi = bi16 if own else bi16 - 8
                xsrc = self.x_own if own else self.x_extra
                xt = xa if bi16 % 2 == 0 else xb2

                def f_ld(sync, xt=xt, xsrc=xsrc, bi=bi, g=cons[bi16 % 2]):
                    if g:
                        sync.wait_ge(sV, g)
                    P.inc(sync.dma_start(out=xt[:], in_=xsrc[bi * 128:(bi + 1) * 128, :]), dS, 16)
                blk.sync(f_ld)
                ld = P.val(dS)

                def f_s1(scalar, xt=xt, ld=ld):
                    scalar.wait_ge(dS, ld)
                    scalar.activation(sdump[:], xt[:], AF.Copy, accum_out=red[:, 16:17])
                    P.inc(scalar.activation(sdump[:], xt[:], AF.Square, accum_out=red[:, 17:18]), sS, 1)
                blk.scalar(f_s1)
                stat = P.val(sS)

                def f_v1(vector, stat=stat):
                    vector.wait_ge(sS, stat)
                    vector.tensor_scalar(red[:, 18:19], red[:, 16:17], 1.0 / 2048.0, None, ALU.mult)
                    P.inc(vector.tensor_scalar(red[:, 19:20], red[:, 17:18], 1.0 / 2048.0, None, ALU.mult), sV, 1)
                    vector.wait_ge(sV, P.val(sV))
                    P.inc(vector.tensor_tensor(red[:, 20:21], red[:, 18:19], red[:, 18:19], ALU.mult), sV, 1)
                    vector.wait_ge(sV, P.val(sV))
                    P.inc(vector.tensor_sub(red[:, 19:20], red[:, 19:20], red[:, 20:21]), sV, 1)
                    vector.wait_ge(sV, P.val(sV))
                    P.inc(vector.tensor_scalar(red[:, 19:20], red[:, 19:20], LN_EPS, None, ALU.add), sV, 1)
                blk.vector(f_v1)
                v1 = P.val(sV)

                def f_s2(scalar, v1=v1):
                    scalar.wait_ge(sV, v1)
                    scalar.activation(red[:, 20:21], red[:, 19:20], AF.Ln)
                    P.inc(scalar.activation(red[:, 21:22], red[:, 20:21], AF.Exp, scale=-0.5), sS, 1)
                blk.scalar(f_s2)
                rstd = P.val(sS)

                def f_v2(vector, xt=xt, rstd=rstd, t_prev=P.val(sT)):
                    vector.wait_ge(sS, rstd)
                    if t_prev:
                        vector.wait_ge(sT, t_prev)
                    vector.tensor_scalar(wrk[:], xt[:], red[:, 18:19], red[:, 21:22],
                                         ALU.subtract, ALU.mult)
                    P.inc(vector.tensor_copy(bft[:], wrk[:]), sV, 1)
                blk.vector(f_v2)
                cons[bi16 % 2] = P.val(sV)
                xn_ok = P.val(sV)


                if not os.environ.get("SKIP_XN_T"):
                    for half in range(2):
                        def mk(half=half, bi16=bi16):
                            def dst(scalar, ps):
                                o = self.xn_fm[:].rearrange("p (ec t) -> p ec t", ec=EC)
                                return scalar.activation(
                                    o[:, half * 8:(half + 1) * 8, bi16 * 128:bi16 * 128 + 128],
                                    ps[:].rearrange("p (j t) -> p j t", j=8), AF.Copy)
                            return dst
                        tiles = [bft[:, (half * 8 + j) * 128:(half * 8 + j + 1) * 128] for j in range(8)]
                        self.transpose_batch(tiles, ps_t, mk(), [(sV, xn_ok)])
                xn_t_done = P.val(sT)

                if own:
                    def f_v3a(vector, bi=bi, xn_t_done=xn_t_done):
                        vector.wait_ge(sT, xn_t_done)
                        P.inc(vector.reduce_max(red[:, 28:29], wrk[:], axis=AX.X,
                                                apply_absolute_value=True), sV, 1)
                        vector.wait_ge(sV, P.val(sV))
                        P.inc(vector.tensor_scalar(red[:, 22:23], red[:, 28:29], 1e-5, None, ALU.max), sV, 1)
                        vector.wait_ge(sV, P.val(sV))
                        P.inc(vector.tensor_scalar(self.rs_all[:, bi:bi + 1], red[:, 22:23],
                                                   1.0 / 127.0, None, ALU.mult), sV, 1)
                    blk.vector(f_v3a)
                    v3a = P.val(sV)

                    def f_s3(scalar, v3a=v3a):
                        scalar.wait_ge(sV, v3a)
                        scalar.activation(red[:, 23:24], red[:, 22:23], AF.Ln)
                        P.inc(scalar.activation(red[:, 24:25], red[:, 23:24], AF.Exp, scale=-1.0), sS, 1)
                    blk.scalar(f_s3)
                    s3 = P.val(sS)

                    def f_v3(vector, s3=s3):
                        vector.wait_ge(sS, s3)
                        P.inc(vector.tensor_scalar(red[:, 25:26], red[:, 24:25], 127.0, None, ALU.mult), sV, 1)
                        vector.wait_ge(sV, P.val(sV))
                        P.inc(vector.tensor_scalar(wrk[:], wrk[:], red[:, 25:26], self.c1285[:],
                                                   ALU.mult, ALU.add), sV, 1)
                    blk.vector(f_v3)
                    self.ladder(wrk, w0b, cbf, 2048, bft, 0.0)
                    lok = P.val(sV)
                    if bi == 0 and "lw0" in self.dbg:
                        def f_dd0(sync, lok=lok):
                            sync.wait_ge(sV, lok)
                            P.inc(sync.dma_start(out=self.dbg["lw0"], in_=bft[:]), dS, 16)
                        blk.sync(f_dd0)

                    for half in range(2):
                        def mk2(half=half):
                            def dst(scalar, ps):
                                return scalar.activation(
                                    xqst[:, half * 1024:(half + 1) * 1024],
                                    ps[:], AF.Copy)
                            return dst
                        wl = [(sV, lok)]
                        if half == 0 and spill_prev:
                            wl.append((dA, spill_prev))
                        tiles = [bft[:, (half * 8 + j) * 128:(half * 8 + j + 1) * 128] for j in range(8)]
                        ev = self.transpose_batch(tiles, ps_t, mk2(), wl)

                    def f_spill(scalar, bi=bi, ev=ev):
                        scalar.wait_ge(sS, ev)
                        src = xqst[:].rearrange("p (ec t) -> p ec t", ec=EC)
                        P.inc(scalar.dma_start(
                            out=self.xq_scr[:, :, bi * 128:(bi + 1) * 128].transpose([1, 0, 2]),
                            in_=src), dA, 16)
                    blk.scalar(f_spill)
                    spill_prev = P.val(dA)

            self.p1_dma = P.val(dA)
        if "xn_fm" in self.dbg or "xq" in self.dbg:
            def f_d(sync):
                sync.wait_ge(sS, P.val(sS))
                sync.wait_ge(dA, P.val(dA))
                if "xn_fm" in self.dbg:
                    P.inc(sync.dma_start(out=self.dbg["xn_fm"], in_=self.xn_fm[:]), dS, 16)
                if "xq" in self.dbg:
                    P.inc(sync.dma_start(out=self.dbg["xq"], in_=self.xq_scr), dS, 16)
            blk.sync(f_d)

    def p2_qkv(self):
        P, blk, sm, nc = self, self.blk, self.sm, self.nc
        dS, dA, sV, sS, sT, sG = (sm[k] for k in ("dS", "dA", "sV", "sS", "sT", "sG"))
        with contextlib.ExitStack() as ph:
            ps_a = ph.enter_context(nc.psum_tensor("ps_a", [128, 512], F32))
            ps_b = ph.enter_context(nc.psum_tensor("ps_b", [128, 512], F32))
            ps_t = ph.enter_context(nc.psum_tensor("ps_t2", [128, 512], BF16))
            wst = ph.enter_context(nc.sbuf_tensor("wst", [128, EC * 256], F32))
            wbf_a = ph.enter_context(nc.sbuf_tensor("wbf_a", [128, EC * 512], BF16))
            wbf_b = ph.enter_context(nc.sbuf_tensor("wbf_b", [128, EC * 512], BF16))
            qn = ph.enter_context(nc.sbuf_tensor("qn", [128, 512], F32))
            tmp = ph.enter_context(nc.sbuf_tensor("tmp", [128, 512], F32))
            qhb = ph.enter_context(nc.sbuf_tensor("qhb", [128, 512], BF16))
            vstage = ph.enter_context(nc.sbuf_tensor("vstage", [128, 8 * 512], BF16))
            qstage = ph.enter_context(nc.sbuf_tensor("qstage", [128, 4 * 1024], BF16))
            self._rope_evac = 0
            self._ps_cons = {id(ps_a): 0, id(ps_b): 0}

            wbf_mm = [0, 0]
            wld_prev = 0
            vspill_prev = 0
            qspill_prev = 0
            for fc in range(12):
                kind = "q" if fc < 4 else ("k" if fc < 8 else "v")
                c0 = fc * 512
                wbf = wbf_a if fc % 2 == 0 else wbf_b
                blocks = range(8) if kind == "q" else range(16)

                cast_half = []
                for half in range(2):
                    wv = cast_half[0] if half else wld_prev

                    def f_wld(sync, c0=c0, half=half, wv=wv):
                        if wv:
                            sync.wait_ge(sV, wv)
                        cc = c0 + half * 256
                        P.inc(sync.dma_start(
                            out=wst[:].rearrange("p (ec f) -> p ec f", ec=EC),
                            in_=self.w_qkv[:, cc:cc + 256].rearrange("(ec p) f -> p ec f", p=128)), dS, 16)
                    blk.sync(f_wld)
                    wld = P.val(dS)

                    def f_cast(vector, wbf=wbf, half=half, wld=wld,
                               mmd=wbf_mm[fc % 2]):
                        vector.wait_ge(dS, wld)
                        if half == 0 and mmd:
                            vector.wait_ge(sT, mmd)
                        dst = wbf[:].rearrange("p (ec f) -> p ec f", ec=EC)
                        P.inc(vector.tensor_copy(
                            dst[:, :, half * 256:half * 256 + 256],
                            wst[:].rearrange("p (ec f) -> p ec f", ec=EC)), sV, 1)
                    blk.vector(f_cast)
                    cast_half.append(P.val(sV))
                wld_prev = P.val(sV)
                cast_ok = P.val(sV)

                for ib, b in enumerate(blocks):
                    ps = ps_a if ib % 2 == 0 else ps_b
                    tpos = b * 128

                    def f_mm(tensor, ps=ps, wbf=wbf, tpos=tpos, cast_ok=cast_ok):
                        tensor.wait_ge(sV, cast_ok)
                        ev = self._ps_cons[id(ps)]
                        if ev:
                            tensor.wait_ge(sV, ev)
                        for ec in range(EC):
                            ins = tensor.matmul(
                                ps[:],
                                self.xn_fm[:, ec * 2048 + tpos: ec * 2048 + tpos + 128],
                                wbf[:, ec * 512:(ec + 1) * 512],
                                start=(ec == 0), stop=(ec == EC - 1))
                        P.inc(ins, sT, 1)
                    blk.tensor(f_mm)
                    mmv = P.val(sT)

                    if kind == "v":
                        def f_ev(vector, ps=ps, b=b, mmv=mmv, g=vspill_prev):
                            vector.wait_ge(sT, mmv)
                            if g and b % 8 == 0:
                                vector.wait_ge(dA, g)
                            P.inc(vector.tensor_copy(vstage[:, (b % 8) * 512:(b % 8 + 1) * 512],
                                                     ps[:]), sV, 1)
                        blk.vector(f_ev)
                        self._ps_cons[id(ps)] = P.val(sV)
                        if b == 7 or b == 15:
                            ev_h = P.val(sV)

                            def f_vsph(scalar, fc=fc, b=b, ev=ev_h):
                                scalar.wait_ge(sV, ev)
                                c = (fc - 8) * 512
                                s0 = b - 7
                                P.inc(scalar.dma_start(
                                    out=self.v_scr[s0:s0 + 8, :, c:c + 512].transpose([1, 0, 2]),
                                    in_=vstage[:].rearrange("p (s f) -> p s f", s=8)), dA, 16)
                            blk.scalar(f_vsph)
                            vspill_prev = P.val(dA)
                    else:
                        self.rope_ln_head4(fc, b, ps, mmv, qn, tmp, qhb, ps_t,
                                           qstage, kind, qspill_prev)
                        self._ps_cons[id(ps)] = P.val(sV)

                if kind == "q":
                    ev = P.val(sV)

                    def f_qsp(scalar, fc=fc, ev=ev):
                        scalar.wait_ge(sV, ev)
                        hh = fc * 4
                        P.inc(scalar.dma_start(
                            out=self.q_scr[hh:hh + 4, :, :].transpose([1, 0, 2]),
                            in_=qstage[:].rearrange("p (h t) -> p h t", h=4)), dA, 16)
                    blk.scalar(f_qsp)
                    qspill_prev = P.val(dA)
                wbf_mm[fc % 2] = P.val(sT)

        self.p2_ok = (P.val(sV), P.val(dA))
        if {"kfm", "qfm", "vtm"} & set(self.dbg):
            def f_d(sync):
                sync.wait_ge(sS, P.val(sS))
                sync.wait_ge(dA, P.val(dA))
                if "kfm" in self.dbg:
                    P.inc(sync.dma_start(out=self.dbg["kfm"], in_=self.k_fm[:]), dS, 16)
                if "qfm" in self.dbg:
                    P.inc(sync.dma_start(out=self.dbg["qfm"], in_=self.q_scr), dS, 16)
                if "vtm" in self.dbg:
                    P.inc(sync.dma_start(out=self.dbg["vtm"], in_=self.v_scr), dS, 16)
            blk.sync(f_d)

    def rope_ln_head4(self, fc, b, ps, mmv, qn, tmp, qhb, ps_t, qstage, kind,
                      qspill_prev):
        P, blk, sm = self, self.blk, self.sm
        sV, sS, sT, sG, dA = (sm[k] for k in ("sV", "sS", "sT", "sG", "dA"))
        red = self.red
        cos = self.cos_t[:, b * 64:(b + 1) * 64]
        sin = self.sin_t[:, b * 64:(b + 1) * 64]
        cosb = cos.unsqueeze(1).broadcast_to([128, 4, 64])
        sinb = sin.unsqueeze(1).broadcast_to([128, 4, 64])

        gp_prev = P.val(sG)

        def f_rope(vector):
            vector.wait_ge(sT, mmv)
            if gp_prev:
                vector.wait_ge(sG, gp_prev)
            p3 = ps[:].rearrange("p (h d) -> p h d", h=4)
            q3 = qn[:].rearrange("p (h d) -> p h d", h=4)
            t3 = tmp[:].rearrange("p (h d) -> p h d", h=4)
            vector.tensor_tensor(q3[:, :, 0:64], p3[:, :, 0:64], cosb, ALU.mult)
            vector.tensor_tensor(t3[:, :, 0:64], p3[:, :, 64:128], sinb, ALU.mult)
            vector.tensor_sub(q3[:, :, 0:64], q3[:, :, 0:64], t3[:, :, 0:64])
            vector.tensor_tensor(q3[:, :, 64:128], p3[:, :, 64:128], cosb, ALU.mult)
            vector.tensor_tensor(t3[:, :, 64:128], p3[:, :, 0:64], sinb, ALU.mult)
            P.inc(vector.tensor_add(q3[:, :, 64:128], q3[:, :, 64:128], t3[:, :, 64:128]), sV, 1)
        blk.vector(f_rope)
        rok = P.val(sV)

        def f_sq(gpsimd):
            gpsimd.wait_ge(sV, rok)
            P.inc(gpsimd.tensor_tensor(tmp[:], qn[:], qn[:], ALU.mult), sG, 1)
        blk.gpsimd(f_sq)
        gok = P.val(sG)

        def f_stats(vector):
            vector.wait_ge(sG, gok)
            q3 = qn[:].rearrange("p (h d) -> p h d", h=4)
            t3 = tmp[:].rearrange("p (h d) -> p h d", h=4)
            vector.tensor_reduce(red[:, 0:4].unsqueeze(2), q3, AX.X, ALU.add)
            P.inc(vector.tensor_reduce(red[:, 4:8].unsqueeze(2), t3, AX.X, ALU.add), sV, 1)
            vector.wait_ge(sV, P.val(sV))
            vector.tensor_scalar(red[:, 0:4], red[:, 0:4], 1.0 / 128.0, None, ALU.mult)
            P.inc(vector.tensor_scalar(red[:, 4:8], red[:, 4:8], 1.0 / 128.0, None, ALU.mult), sV, 1)
            vector.wait_ge(sV, P.val(sV))
            P.inc(vector.tensor_tensor(red[:, 8:12], red[:, 0:4], red[:, 0:4], ALU.mult), sV, 1)
            vector.wait_ge(sV, P.val(sV))
            P.inc(vector.tensor_sub(red[:, 4:8], red[:, 4:8], red[:, 8:12]), sV, 1)
            vector.wait_ge(sV, P.val(sV))
            P.inc(vector.tensor_scalar(red[:, 4:8], red[:, 4:8], LN_EPS, None, ALU.add), sV, 1)
        blk.vector(f_stats)
        v2 = P.val(sV)

        def f_rstd(scalar):
            scalar.wait_ge(sV, v2)
            scalar.activation(red[:, 8:12], red[:, 4:8], AF.Ln)
            P.inc(scalar.activation(red[:, 12:16], red[:, 8:12], AF.Exp, scale=-0.5), sS, 1)
        blk.scalar(f_rstd)
        s2 = P.val(sS)

        t_prev = P.val(sT)

        def f_app(vector):
            vector.wait_ge(sS, s2)
            if t_prev:
                vector.wait_ge(sT, t_prev)
            for hh in range(4):
                ins = vector.tensor_scalar(qhb[:, hh * 128:(hh + 1) * 128],
                                           qn[:, hh * 128:(hh + 1) * 128],
                                           red[:, 0 + hh:1 + hh], red[:, 12 + hh:13 + hh],
                                           ALU.subtract, ALU.mult)
            P.inc(ins, sV, 1)
        blk.vector(f_app)
        appok = P.val(sV)

        evac_prev = self._rope_evac

        def f_t(tensor):
            tensor.wait_ge(sV, appok)
            if evac_prev:
                tensor.wait_ge(sV, evac_prev)
            for hh in range(4):
                ins = tensor.transpose(ps_t[:, hh * 128:(hh + 1) * 128],
                                       qhb[:, hh * 128:(hh + 1) * 128], self.eyeb[:])
            P.inc(ins, sT, 1)
        blk.tensor(f_t)
        tok = P.val(sT)
        h0 = (fc % 4) * 4

        def f_ev(vector):
            vector.wait_ge(sT, tok)
            if kind == "q" and b == 0 and qspill_prev:
                vector.wait_ge(dA, qspill_prev)
            src = ps_t[:].rearrange("p (h t) -> p h t", h=4)
            if kind == "k":
                o = self.k_fm[:].rearrange("p (h t) -> p h t", h=H)
                dst = o[:, h0:h0 + 4, b * 128:b * 128 + 128]
            else:
                o = qstage[:].rearrange("p (h t) -> p h t", h=4)
                dst = o[:, :, b * 128:b * 128 + 128]
            P.inc(vector.tensor_scalar(dst, src, self.lng_c[:], self.lnb_c[:],
                                       ALU.mult, ALU.add), sV, 1)
        blk.vector(f_ev)
        self._rope_evac = P.val(sV)

    def p34_tau(self, tau):
        P, blk, sm, nc = self, self.blk, self.sm, self.nc
        dS, dA, sV, sS, sT, sG = (sm[k] for k in ("dS", "dA", "sV", "sS", "sT", "sG"))
        slots = SLOTS[tau]
        ns = len(slots)
        with contextlib.ExitStack() as ph:
            ps_s = [ph.enter_context(nc.psum_tensor(f"ps_s{i}_{tau}", [128, 512], F32)) for i in range(2)]
            ps_dn = ph.enter_context(nc.psum_tensor(f"ps_dn_{tau}", [16, 512], F32))
            ps_o = [ph.enter_context(nc.psum_tensor(f"ps_o{i}_{tau}", [128, 128], F32)) for i in range(4)]
            ps_t = ph.enter_context(nc.psum_tensor(f"ps_t3_{tau}", [128, 1024], BF16))
            o_fm = ph.enter_context(nc.sbuf_tensor(f"o_fm_{tau}", [128, H * 512], BF16))
            pha = contextlib.ExitStack()
            q_sb = pha.enter_context(nc.sbuf_tensor(f"q_sb_{tau}", [128, H * 512], BF16))
            v_sb = [pha.enter_context(nc.sbuf_tensor(f"v_sb{i}_{tau}", [128, E], BF16)) for i in range(2)]
            m_sb = [pha.enter_context(nc.sbuf_tensor(f"m_sb{i}_{tau}", [128, 128], BF16)) for i in range(2)]
            s2_sb = [pha.enter_context(nc.sbuf_tensor(f"s2_{i}_{tau}", [128, 512], F32)) for i in range(2)]
            t2_sb = [pha.enter_context(nc.sbuf_tensor(f"t2_{i}_{tau}", [128, 512], F32)) for i in range(2)]
            a_bf = [pha.enter_context(nc.sbuf_tensor(f"a_bf{i}_{tau}", [128, 512], BF16)) for i in range(2)]
            o_sb = pha.enter_context(nc.sbuf_tensor(f"o_sb_{tau}", [128, 4 * 2048], BF16))
            dn_sb = pha.enter_context(nc.sbuf_tensor(f"dn_sb_{tau}", [16, 512], F32))
            rc_sb = pha.enter_context(nc.sbuf_tensor(f"rc_sb_{tau}", [128, 64], F32))
            self._pt_evac = 0

            def f_ql(sync):
                sync.wait_ge(sV, self.p2_ok[0])
                sync.wait_ge(dA, self.p2_ok[1])
                P.inc(sync.dma_start(out=q_sb[:].rearrange("p (h t) -> p h t", h=H),
                                     in_=self.q_scr[:, :, tau * 512:(tau + 1) * 512].transpose([1, 0, 2])), dS, 16)
            blk.sync(f_ql)
            q_ok = P.val(dS)

            ents = []
            for s in slots:
                c0 = 128 * max(0, (s - 4 * tau) if s < 8 else (s - 8 - 4 * tau))
                ents.append((c0, s))
            ents.sort()
            qc_stop = {qc: max(i for i, (c0, _) in enumerate(ents)
                               if c0 <= 128 * qc) for qc in range(4)}

            a_cons = {0: 0, 1: 0}
            s2_cons = {0: 0, 1: 0}
            ab_cons = {0: 0, 1: 0}
            evac_o = 0
            for hi in range(H):
                for si, (c0, s) in enumerate(ents):
                    par = (hi * ns + si) % 2
                    qc0 = c0 // 128

                    def f_vl(sync, par=par, s=s, c0=c0, g=a_cons[par]):
                        if g:
                            sync.wait_ge(sT, g)
                        P.inc(sync.dma_start(out=v_sb[par][:], in_=self.v_scr[s, :, :]), dS, 16)
                        P.inc(sync.dma_start(out=m_sb[par][:],
                                             in_=self.masks[tau, s, :, c0:c0 + 128]), dS, 16)
                    blk.sync(f_vl)
                    vm = P.val(dS)

                    first = (hi == 0 and si == 0)
                    last = (hi == H - 1 and si == ns - 1)

                    def f_st(tensor, hi=hi, s=s, par=par, c0=c0, g=s2_cons[par]):
                        tensor.wait_ge(dS, q_ok)
                        if g:
                            tensor.wait_ge(sV, g)
                        P.inc(tensor.matmul(ps_s[par][:, c0:512],
                                            self.k_fm[:, hi * 2048 + slot_tpos(s):
                                                       hi * 2048 + slot_tpos(s) + 128],
                                            q_sb[:, hi * 512 + c0:(hi + 1) * 512],
                                            start=True, stop=True), sT, 1)
                    blk.tensor(f_st)
                    st_ok = P.val(sT)

                    def f_sqm(vector, par=par, c0=c0, st_ok=st_ok, vm=vm,
                              gp_prev=P.val(sG)):
                        vector.wait_ge(sT, st_ok)
                        vector.wait_ge(dS, vm)
                        if gp_prev > 1:
                            vector.wait_ge(sG, gp_prev - 1)
                        vector.tensor_copy(s2_sb[par][:, c0:512], ps_s[par][:, c0:512])
                        P.inc(vector.tensor_tensor(s2_sb[par][:, c0:c0 + 128],
                                                   s2_sb[par][:, c0:c0 + 128],
                                                   m_sb[par][:], ALU.mult), sV, 1)
                    blk.vector(f_sqm)
                    s2_cons[par] = P.val(sV)
                    mok = P.val(sV)

                    def f_sq2(gpsimd, par=par, c0=c0, mok=mok, g=ab_cons[par]):
                        gpsimd.wait_ge(sV, mok)
                        if g:
                            gpsimd.wait_ge(sV, g)
                        P.inc(gpsimd.tensor_tensor(t2_sb[par][:, c0:512],
                                                   s2_sb[par][:, c0:512],
                                                   s2_sb[par][:, c0:512], ALU.mult), sG, 1)
                    blk.gpsimd(f_sq2)
                    gok = P.val(sG)

                    def f_ab(vector, par=par, c0=c0, gok=gok, g=a_cons[par]):
                        vector.wait_ge(sG, gok)
                        if g:
                            vector.wait_ge(sT, g)
                        P.inc(vector.tensor_tensor(a_bf[par][:, c0:512],
                                                   t2_sb[par][:, c0:512],
                                                   t2_sb[par][:, c0:512], ALU.mult), sV, 1)
                    blk.vector(f_ab)
                    ab_cons[par] = P.val(sV)
                    aok = P.val(sV)

                    def f_av(tensor, par=par, hi=hi, first=first, last=last,
                             aok=aok, vm=vm, si=si, c0=c0, qc0=qc0, eo=evac_o):
                        tensor.wait_ge(sV, aok)
                        tensor.wait_ge(dS, vm)
                        if si == 0 and eo:
                            tensor.wait_ge(sS, eo)
                        tensor.matmul(ps_dn[:, c0:512], self.ehb[:, hi * 16:(hi + 1) * 16],
                                      a_bf[par][:, c0:512], start=first, stop=last)
                        for qc in range(qc0, 4):
                            ins = tensor.matmul(ps_o[qc][:],
                                                a_bf[par][:, qc * 128:(qc + 1) * 128],
                                                v_sb[par][:, hi * 128:(hi + 1) * 128],
                                                start=(si == 0), stop=(si == qc_stop[qc]))
                        P.inc(ins, sT, 1)
                    blk.tensor(f_av)
                    a_cons[par] = P.val(sT)
                av_ok = P.val(sT)

                def f_oev(scalar, hi=hi, av_ok=av_ok):
                    scalar.wait_ge(sT, av_ok)
                    for qc in range(4):
                        ins = scalar.activation(
                            o_sb[:, qc * 2048 + hi * 128: qc * 2048 + hi * 128 + 128],
                            ps_o[qc][:], AF.Copy)
                    P.inc(ins, sS, 1)
                blk.scalar(f_oev)
                evac_o = P.val(sS)

            def f_dnev(vector):
                vector.wait_ge(sT, P.val(sT))
                vector.wait_ge(sS, evac_o)
                P.inc(vector.tensor_copy(dn_sb[:], ps_dn[:]), sV, 1)
            blk.vector(f_dnev)
            dn_ok = P.val(sV)

            def f_dnt(tensor):
                tensor.wait_ge(sV, dn_ok)
                for qc in range(4):
                    ins = tensor.transpose(ps_o[qc][:, 0:16],
                                           dn_sb[:, qc * 128:(qc + 1) * 128],
                                           self.eyef[0:16, 0:16])
                P.inc(ins, sT, 1)
            blk.tensor(f_dnt)
            dnt_ok = P.val(sT)

            def f_rc1(vector):
                vector.wait_ge(sT, dnt_ok)
                for qc in range(4):
                    ins = vector.tensor_copy(rc_sb[:, qc * 16:(qc + 1) * 16],
                                             ps_o[qc][:, 0:16])
                P.inc(ins, sV, 1)
            blk.vector(f_rc1)
            rc1 = P.val(sV)

            def f_rc2(scalar):
                scalar.wait_ge(sV, rc1)
                scalar.activation(rc_sb[:], rc_sb[:], AF.Ln)
                P.inc(scalar.activation(rc_sb[:], rc_sb[:], AF.Exp, scale=-1.0), sS, 1)
            blk.scalar(f_rc2)
            rc_ok = P.val(sS)

            def f_nrm(vector):
                vector.wait_ge(sS, rc_ok)
                for qc in range(4):
                    for hh in range(16):
                        ins = vector.tensor_scalar(
                            o_sb[:, qc * 2048 + hh * 128: qc * 2048 + hh * 128 + 128],
                            o_sb[:, qc * 2048 + hh * 128: qc * 2048 + hh * 128 + 128],
                            rc_sb[:, qc * 16 + hh: qc * 16 + hh + 1], None, ALU.mult)
                P.inc(ins, sV, 1)
            blk.vector(f_nrm)
            nrm_ok = P.val(sV)

            for batch in range(8):
                def mk(batch=batch):
                    def dst(scalar, ps):
                        for j in range(8):
                            k = batch * 8 + j
                            qc, hh = divmod(k, 16)
                            ins = scalar.activation(
                                o_fm[:, hh * 512 + qc * 128: hh * 512 + qc * 128 + 128],
                                ps[:, j * 128:(j + 1) * 128], AF.Copy)
                        return ins
                    return dst
                tiles = []
                for j in range(8):
                    k = batch * 8 + j
                    qc, hh = divmod(k, 16)
                    tiles.append(o_sb[:, qc * 2048 + hh * 128: qc * 2048 + hh * 128 + 128])
                self.transpose_batch(tiles, ps_t, mk(), [(sV, nrm_ok)])
            o_ready = P.val(sS)
            pha.close()

            ps_w = [ps_s[0], ps_s[1]]
            wst = ph.enter_context(nc.sbuf_tensor(f"wo_st_{tau}", [128, 16 * 512], F32))
            wbf = ph.enter_context(nc.sbuf_tensor(f"wo_bf_{tau}", [128, 16 * 512], BF16))
            cast_prev = 0
            evp = {0: 0, 1: 0}
            for ec in range(4):
                def f_ld(sync, ec=ec, g=cast_prev):
                    if ec == 0:
                        sync.wait_ge(sS, o_ready)
                    if g:
                        sync.wait_ge(sV, g)
                    P.inc(sync.dma_start(
                        out=wst[:].rearrange("p (c f) -> p c f", c=16),
                        in_=self.w_out[:, ec * 512:(ec + 1) * 512]
                            .rearrange("(c p) f -> p c f", p=128)), dS, 16)
                blk.sync(f_ld)
                wld = P.val(dS)
                mm_prev = P.val(sT)

                def f_c(vector, wld=wld, mm_prev=mm_prev, ec=ec):
                    vector.wait_ge(dS, wld)
                    if ec > 0:
                        vector.wait_ge(sT, mm_prev)
                    P.inc(vector.tensor_copy(wbf[:], wst[:]), sV, 1)
                blk.vector(f_c)
                cast_prev = P.val(sV)
                c_ok = P.val(sV)

                for tb in range(4):
                    par = tb % 2

                    def f_mm(tensor, tb=tb, par=par, c_ok=c_ok, ev=evp[par]):
                        tensor.wait_ge(sV, c_ok)
                        tensor.wait_ge(sS, o_ready)
                        if ev:
                            tensor.wait_ge(sV, ev)
                        for hc in range(16):
                            ins = tensor.matmul(ps_w[par][:],
                                                o_fm[:, hc * 512 + tb * 128:
                                                     hc * 512 + tb * 128 + 128],
                                                wbf[:, hc * 512:(hc + 1) * 512],
                                                start=(hc == 0), stop=(hc == 15))
                        P.inc(ins, sT, 1)
                    blk.tensor(f_mm)
                    mm_ok = P.val(sT)

                    def f_ev(vector, tau=tau, tb=tb, ec=ec, par=par, mm_ok=mm_ok):
                        vector.wait_ge(sT, mm_ok)
                        P.inc(vector.tensor_copy(
                            self.s_tm[:, (tau * 4 + tb) * 2048 + ec * 512:
                                      (tau * 4 + tb) * 2048 + ec * 512 + 512],
                            ps_w[par][:]), sV, 1)
                    blk.vector(f_ev)
                    evp[par] = P.val(sV)
        self.s_ready = P.val(sV)
        if tau == 1 and "s_tm" in self.dbg:
            def f_d(sync):
                sync.wait_ge(sV, self.s_ready)
                P.inc(sync.dma_start(out=self.dbg["s_tm"], in_=self.s_tm[:]), dS, 16)
            blk.sync(f_d)

    def p5_ffn(self):
        P, blk, sm, nc = self, self.blk, self.sm, self.nc
        dS, dA, dO, sV, sS, sT, sG, sC = (sm[k] for k in ("dS", "dA", "dO", "sV", "sS", "sT", "sG", "sC"))
        red = self.red
        ssq = self.ssq

        with contextlib.ExitStack() as ph:
            ps1 = [ph.enter_context(nc.psum_tensor(f"ps5a{i}", [128, 256], F32)) for i in range(4)]
            xq_sb = ph.enter_context(nc.sbuf_tensor("xq_sb", [128, EC * 1024], BF16))
            w1st = ph.enter_context(nc.sbuf_tensor("w1st", [128, EC * 256], F32))
            w3st = ph.enter_context(nc.sbuf_tensor("w3st", [128, EC * 256], F32))
            w1bf = ph.enter_context(nc.sbuf_tensor("w1bf", [128, EC * 256], BF16))
            w3bf = ph.enter_context(nc.sbuf_tensor("w3bf", [128, EC * 256], BF16))
            gst = ph.enter_context(nc.sbuf_tensor("gst", [128, 8 * 256], F32))
            t1 = ph.enter_context(nc.sbuf_tensor("t1_5a", [128, 256], F32))
            t3 = ph.enter_context(nc.sbuf_tensor("t3_5a", [128, 256], F32))
            sqd = ph.enter_context(nc.sbuf_tensor("sqd", [128, 256], F32))

            def f_xql(sync):
                sync.wait_ge(dA, self.p1_dma)
                P.inc(sync.dma_start(out=xq_sb[:].rearrange("p (e t) -> p e t", e=EC),
                                     in_=self.xq_scr.transpose([1, 0, 2])), dS, 16)
            blk.sync(f_xql)
            xq_ok = P.val(dS)

            def f_z(vector):
                P.inc(vector.memset(ssq[:], 0.0), sV, 1)
            blk.vector(f_z)

            cast_prev = 0
            mm_prev = 0
            gcons = {0: 0, 2: 0}
            ss_prev = 0
            gspill_prev = 0
            for ci in range(22):
                def f_ld(sync, ci=ci, g=cast_prev):
                    if g:
                        sync.wait_ge(sV, g)
                    P.inc(sync.dma_start(
                        out=w1st[:].rearrange("p (e f) -> p e f", e=EC),
                        in_=self.w13T[:, ci * 256:(ci + 1) * 256]
                            .rearrange("(e p) f -> p e f", p=128)), dS, 16)
                    P.inc(sync.dma_start(
                        out=w3st[:].rearrange("p (e f) -> p e f", e=EC),
                        in_=self.w13T[:, HID + ci * 256: HID + (ci + 1) * 256]
                            .rearrange("(e p) f -> p e f", p=128)), dS, 16)
                blk.sync(f_ld)
                wld = P.val(dS)

                def f_c(vector, wld=wld, mm_prev=mm_prev):
                    vector.wait_ge(dS, wld)
                    if mm_prev:
                        vector.wait_ge(sT, mm_prev)
                    vector.tensor_copy(w1bf[:], w1st[:])
                    P.inc(vector.tensor_copy(w3bf[:], w3st[:]), sV, 1)
                blk.vector(f_c)
                cast_prev = P.val(sV)
                c_ok = P.val(sV)

                for bi in range(8):
                    pp = (bi % 2) * 2

                    def f_mm(tensor, bi=bi, pp=pp, c_ok=c_ok, g=gcons[pp]):
                        tensor.wait_ge(sV, c_ok)
                        tensor.wait_ge(dS, xq_ok)
                        if g:
                            tensor.wait_ge(sV, g)
                        for ec in range(EC):
                            tensor.matmul(ps1[pp][:],
                                          xq_sb[:, ec * 1024 + bi * 128: ec * 1024 + bi * 128 + 128],
                                          w1bf[:, ec * 256:(ec + 1) * 256],
                                          start=(ec == 0), stop=(ec == EC - 1))
                        for ec in range(EC):
                            ins = tensor.matmul(ps1[pp + 1][:],
                                                xq_sb[:, ec * 1024 + bi * 128: ec * 1024 + bi * 128 + 128],
                                                w3bf[:, ec * 256:(ec + 1) * 256],
                                                start=(ec == 0), stop=(ec == EC - 1))
                        P.inc(ins, sT, 1)
                    blk.tensor(f_mm)
                    mmv = P.val(sT)

                    def f_g1(vector, bi=bi, pp=pp, mmv=mmv, gp_prev=P.val(sG)):
                        vector.wait_ge(sT, mmv)
                        if gp_prev:
                            vector.wait_ge(sG, gp_prev)
                        vector.tensor_scalar(t1[:], ps1[pp][:], self.rs_all[:, bi:bi + 1],
                                             self.c0[:], ALU.mult, ALU.max)
                        P.inc(vector.tensor_scalar(t3[:], ps1[pp + 1][:],
                                                   self.rs_all[:, bi:bi + 1], None, ALU.mult), sV, 1)
                    blk.vector(f_g1)
                    gcons[pp] = P.val(sV)
                    g1 = P.val(sV)

                    def f_g2(gpsimd, bi=bi, g1=g1, gsp=gspill_prev):
                        gpsimd.wait_ge(sV, g1)
                        if gsp and bi == 0:
                            gpsimd.wait_ge(dA, gsp)
                        gpsimd.tensor_tensor(t1[:], t1[:], t1[:], ALU.mult)
                        P.inc(gpsimd.tensor_tensor(gst[:, bi * 256:(bi + 1) * 256],
                                                   t1[:], t3[:], ALU.mult), sG, 1)
                    blk.gpsimd(f_g2)
                    g2 = P.val(sG)

                    def f_ss(scalar, bi=bi, g2=g2, ssp=ss_prev):
                        scalar.wait_ge(sG, g2)
                        if ssp:
                            scalar.wait_ge(sV, ssp)
                        P.inc(scalar.activation(sqd[:], gst[:, bi * 256:(bi + 1) * 256],
                                                AF.Square, accum_out=red[:, 16:17]), sS, 1)
                    blk.scalar(f_ss)
                    ss_ok = P.val(sS)

                    def f_ssa(vector, bi=bi, ss_ok=ss_ok):
                        vector.wait_ge(sS, ss_ok)
                        P.inc(vector.tensor_add(ssq[:, bi:bi + 1], ssq[:, bi:bi + 1],
                                                red[:, 16:17]), sV, 1)
                    blk.vector(f_ssa)
                    ss_prev = P.val(sV)

                g_all = P.val(sG)

                def f_gsp(scalar, ci=ci, g_all=g_all):
                    scalar.wait_ge(sG, g_all)
                    P.inc(scalar.dma_start(
                        out=self.g_scr[:, :, ci * 256:(ci + 1) * 256].transpose([1, 0, 2]),
                        in_=gst[:].rearrange("p (b f) -> p b f", b=8)), dA, 16)
                blk.scalar(f_gsp)
                gspill_prev = P.val(dA)
                mm_prev = P.val(sT)
            p5a_dma = P.val(dA)
            p5a_ssq = P.val(sV)

        if "gated" in self.dbg:
            def f_d(sync):
                sync.wait_ge(dA, p5a_dma)
                P.inc(sync.dma_start(out=self.dbg["gated"], in_=self.g_scr), dS, 16)
            blk.sync(f_d)
        self.fence("p5a")

        with contextlib.ExitStack() as ph:
            ps_t = ph.enter_context(nc.psum_tensor("ps_t5", [128, 1024], BF16))
            nwb = ph.enter_context(nc.sbuf_tensor("nwb", [128, HID], F32))
            ga = ph.enter_context(nc.sbuf_tensor("ga", [128, HID], F32))
            w0b = ph.enter_context(nc.sbuf_tensor("w0b5", [128, HID], F32))
            cb5 = ph.enter_context(nc.sbuf_tensor("cb5", [128, HID], F32))
            iqb = ph.enter_context(nc.sbuf_tensor("iqb", [128, HID], BF16))
            iqs = ph.enter_context(nc.sbuf_tensor("iqs", [128, HC * 128], BF16))
            w2st5 = ph.enter_context(nc.sbuf_tensor("w2st5", [128, E], F32))
            w2cb5 = ph.enter_context(nc.sbuf_tensor("w2cb5", [128, E], BF16))
            z2k = ph.enter_context(nc.sbuf_tensor("z2k", [128, E], F32))
            self._pt_evac = 0

            def f_z2(vector):
                P.inc(vector.memset(z2k[:], 0.0), sV, 1)
            blk.vector(f_z2)
            z2_ok = P.val(sV)
            self._w2pc = (0, 0, 0)

            def f_nw(sync):
                P.inc(sync.dma_start(out=nwb[:],
                                     in_=self.norm_w.unsqueeze(0).broadcast_to([128, HID])), dS, 16)
            blk.sync(f_nw)
            nw_ok = P.val(dS)

            iqsp_prev = 0
            for bi in range(8):
                def f_gl(sync, bi=bi, g=P.val(sV)):
                    if g:
                        sync.wait_ge(sV, g)
                    P.inc(sync.dma_start(out=ga[:], in_=self.g_scr[bi, :, :]), dS, 16)
                blk.sync(f_gl)
                gld = P.val(dS)

                def f_gn(gpsimd, gld=gld):
                    gpsimd.wait_ge(dS, gld)
                    gpsimd.wait_ge(dS, nw_ok)
                    P.inc(gpsimd.tensor_tensor(ga[:], ga[:], nwb[:], ALU.mult), sG, 1)
                blk.gpsimd(f_gn)
                gn_ok = P.val(sG)

                def f_r1(vector, bi=bi, gn_ok=gn_ok, t_prev=P.val(sT)):
                    vector.wait_ge(sG, gn_ok)
                    vector.wait_ge(sV, p5a_ssq)
                    if t_prev:
                        vector.wait_ge(sT, t_prev)
                    P.inc(vector.reduce_max(red[:, 28:29], ga[:], axis=AX.X,
                                            apply_absolute_value=True), sV, 1)
                    vector.wait_ge(sV, P.val(sV))
                    vector.tensor_scalar(red[:, 22:23], red[:, 28:29], 1.0, None, ALU.mult)
                    P.inc(vector.tensor_scalar(red[:, 19:20], ssq[:, bi:bi + 1],
                                               1.0 / HID, NORM_EPS, ALU.mult, ALU.add), sV, 1)
                blk.vector(f_r1)
                r1 = P.val(sV)

                def f_r2(scalar, r1=r1):
                    scalar.wait_ge(sV, r1)
                    scalar.activation(red[:, 20:21], red[:, 19:20], AF.Ln)
                    P.inc(scalar.activation(red[:, 21:22], red[:, 20:21], AF.Exp, scale=-0.5), sS, 1)
                blk.scalar(f_r2)
                r2 = P.val(sS)

                def f_r3a(vector, bi=bi, r2=r2):
                    vector.wait_ge(sS, r2)
                    P.inc(vector.tensor_tensor(red[:, 25:26], red[:, 21:22], red[:, 22:23], ALU.mult), sV, 1)
                    vector.wait_ge(sV, P.val(sV))
                    P.inc(vector.tensor_scalar(red[:, 25:26], red[:, 25:26], 1e-5, None, ALU.max), sV, 1)
                    vector.wait_ge(sV, P.val(sV))
                    P.inc(vector.tensor_scalar(self.r2_all[:, bi:bi + 1], red[:, 25:26],
                                               1.0 / 127.0, None, ALU.mult), sV, 1)
                blk.vector(f_r3a)
                r3a = P.val(sV)

                def f_r3b(scalar, r3a=r3a):
                    scalar.wait_ge(sV, r3a)
                    scalar.activation(red[:, 26:27], red[:, 25:26], AF.Ln)
                    P.inc(scalar.activation(red[:, 26:27], red[:, 26:27], AF.Exp, scale=-1.0), sS, 1)
                blk.scalar(f_r3b)
                r3b = P.val(sS)

                def f_r3(vector, bi=bi, r3b=r3b):
                    vector.wait_ge(sS, r3b)
                    P.inc(vector.tensor_tensor(red[:, 27:28], red[:, 21:22], red[:, 26:27], ALU.mult), sV, 1)
                    vector.wait_ge(sV, P.val(sV))
                    P.inc(vector.tensor_scalar(red[:, 27:28], red[:, 27:28], 127.0, None, ALU.mult), sV, 1)
                    vector.wait_ge(sV, P.val(sV))
                    P.inc(vector.tensor_scalar(ga[:], ga[:], red[:, 27:28], self.c1285[:],
                                               ALU.mult, ALU.add), sV, 1)
                blk.vector(f_r3)
                self.ladder(ga, w0b, cb5, HID, iqb, 0.0)
                lok = P.val(sV)

                for batch in range(6):
                    j0 = batch * 8
                    jn = min(8, HC - j0)
                    if jn <= 0:
                        break

                    def mk(j0=j0, jn=jn):
                        def dst(scalar, ps):
                            return scalar.activation(iqs[:, j0 * 128:(j0 + jn) * 128],
                                                     ps[:, 0:jn * 128], AF.Copy)
                        return dst
                    wl = [(sV, lok)]
                    if batch == 0 and iqsp_prev:
                        wl.append((dA, iqsp_prev))
                    tiles = [iqb[:, (j0 + j) * 128:(j0 + j + 1) * 128] for j in range(jn)]
                    self.transpose_batch(tiles, ps_t, mk(), wl)
                iq_ev = P.val(sS)

                def f_iqs(scalar, bi=bi, iq_ev=iq_ev):
                    scalar.wait_ge(sS, iq_ev)
                    P.inc(scalar.dma_start(
                        out=self.iq_scr[:, :, bi * 128:(bi + 1) * 128].transpose([1, 0, 2]),
                        in_=iqs[:].rearrange("p (h t) -> p h t", h=HC)), dA, 16)
                blk.scalar(f_iqs)
                iqsp_prev = P.val(dA)

                j0, j1 = (HC * bi) // 8, (HC * (bi + 1)) // 8
                for j in range(j0, j1):
                    pin, pc, pout = self._w2pc

                    def f_wi(sync, j=j, pc=pc):
                        if pc:
                            sync.wait_ge(sC, pc)
                        P.inc(sync.dma_start(out=w2st5[:],
                                             in_=self.w2T[j * 128:(j + 1) * 128, :]), dS, 16)
                    blk.sync(f_wi)
                    win = P.val(dS)

                    def f_wc(gpsimd, win=win, pout=pout, z2_ok=z2_ok):
                        gpsimd.wait_ge(dS, win)
                        gpsimd.wait_ge(sV, z2_ok)
                        if pout:
                            gpsimd.wait_ge(dA, pout)
                        P.inc(gpsimd.tensor_tensor(w2cb5[:], w2st5[:], z2k[:],
                                                   ALU.add), sC, 1)
                    blk.gpsimd(f_wc)
                    wc = P.val(sC)

                    def f_wo(scalar, j=j, wc=wc):
                        scalar.wait_ge(sC, wc)
                        P.inc(scalar.dma_start(out=self.w2b_scr[j * 128:(j + 1) * 128, :],
                                               in_=w2cb5[:]), dA, 16)
                    blk.scalar(f_wo)
                    self._w2pc = (win, wc, P.val(dA))
            p5b_dma = P.val(dA)

        if "iq" in self.dbg:
            def f_d(sync):
                sync.wait_ge(dA, p5b_dma)
                P.inc(sync.dma_start(out=self.dbg["iq"], in_=self.iq_scr), dS, 16)
            blk.sync(f_d)
        if "scales" in self.dbg:
            def f_ds(sync):
                sync.wait_ge(sV, P.val(sV))
                sync.wait_ge(sS, P.val(sS))
                P.inc(sync.dma_start(out=self.dbg["scales"][:, 0:8], in_=self.r2_all[:]), dS, 16)
                P.inc(sync.dma_start(out=self.dbg["scales"][:, 8:16], in_=self.rs_all[:]), dS, 16)
                P.inc(sync.dma_start(out=self.dbg["scales"][:, 16:24], in_=self.ssq[:]), dS, 16)
            blk.sync(f_ds)
        self.fence("p5b")

        self.es_s.close()
        with contextlib.ExitStack() as ph:
            ps5 = [ph.enter_context(nc.psum_tensor(f"ps5c{i}", [128, 512], F32)) for i in range(4)]
            iqt_all = ph.enter_context(nc.sbuf_tensor("iqt_all", [128, HC * 512], BF16))
            w2a = [ph.enter_context(nc.sbuf_tensor(f"w2a{i}", [128, HC * 512], BF16)) for i in range(2)]
            stl = ph.enter_context(nc.sbuf_tensor("stl", [128, 4 * 2048], F32))
            xr = [ph.enter_context(nc.sbuf_tensor(f"xr{i}", [128, 4 * 512], F32)) for i in range(2)]
            gb = ph.enter_context(nc.sbuf_tensor("gb", [128, 512], F32))
            ob = ph.enter_context(nc.sbuf_tensor("ob", [128, 4 * 512], F32))
            if "ffn" in self.dbg:
                fstage = ph.enter_context(nc.sbuf_tensor("fstage", [128, 4 * 512], F32))

            mmW = {0: 0, 1: 0}
            evP = {0: 0, 1: 0, 2: 0, 3: 0}
            xrW = {0: 0, 1: 0}
            outPrev = 0
            for tg in range(2):
                blks = [tg * 4 + i for i in range(4)]

                def f_iql(sync, tg=tg, g=P.val(sT)):
                    if tg and g:
                        sync.wait_ge(sT, g)
                    P.inc(sync.dma_start(
                        out=iqt_all[:].rearrange("p (h f) -> p h f", h=HC),
                        in_=self.iq_scr[:, :, tg * 512:(tg + 1) * 512].transpose([1, 0, 2])), dS, 16)
                blk.sync(f_iql)
                iq_ok = P.val(dS)

                def f_sl(scalar, tg=tg, g=P.val(sV)):
                    if tg and g:
                        scalar.wait_ge(sV, g)
                    P.inc(scalar.dma_start(
                        out=stl[:],
                        in_=self.s_scr[:, tg * 4 * 2048:(tg + 1) * 4 * 2048]), dA, 16)
                blk.scalar(f_sl)
                sl_ok = P.val(dA)

                for ec in range(4):
                    par = ec % 2

                    def f_wld(sync, ec=ec, par=par, g=mmW[par]):
                        if g:
                            sync.wait_ge(sT, g)
                        P.inc(sync.dma_start(
                            out=w2a[par][:].rearrange("p (h f) -> p h f", h=HC),
                            in_=self.w2b_scr[:, ec * 512:(ec + 1) * 512]
                                .rearrange("(h p) f -> p h f", p=128)), dS, 16)
                    blk.sync(f_wld)
                    wld = P.val(dS)

                    def f_xle(scalar, ec=ec, par=par, blks=blks, g=xrW[par]):
                        if g:
                            scalar.wait_ge(sV, g)
                        for i, bi in enumerate(blks):
                            P.inc(scalar.dma_start(
                                out=xr[par][:, i * 512:(i + 1) * 512],
                                in_=self.x_own[bi * 128:(bi + 1) * 128,
                                               ec * 512:(ec + 1) * 512]), dA, 16)
                    blk.scalar(f_xle)
                    x_ok = P.val(dA)

                    for i in range(4):
                        def f_mm(tensor, i=i, par=par, wld=wld, iq_ok=iq_ok,
                                 ev=evP[i]):
                            tensor.wait_ge(dS, wld)
                            tensor.wait_ge(dS, iq_ok)
                            if ev:
                                tensor.wait_ge(sV, ev)
                            for hc in range(HC):
                                ins = tensor.matmul(
                                    ps5[i][:],
                                    iqt_all[:, hc * 512 + i * 128: hc * 512 + i * 128 + 128],
                                    w2a[par][:, hc * 512:(hc + 1) * 512],
                                    start=(hc == 0), stop=(hc == HC - 1))
                            P.inc(ins, sT, 1)
                        blk.tensor(f_mm)
                    mmW[par] = P.val(sT)
                    mm_done = P.val(sT)

                    if "ffn" in self.dbg:
                        for i in range(4):
                            def f_fd(vector, i=i, mm_done=mm_done):
                                vector.wait_ge(sT, mm_done)
                                P.inc(vector.tensor_copy(fstage[:, i * 512:(i + 1) * 512],
                                                         ps5[i][:]), sV, 1)
                            blk.vector(f_fd)
                        fdv = P.val(sV)

                        def f_fdma(sync, ec=ec, blks=blks, fdv=fdv):
                            sync.wait_ge(sV, fdv)
                            for i, bi in enumerate(blks):
                                P.inc(sync.dma_start(
                                    out=self.dbg["ffn"][bi, :, ec * 512:(ec + 1) * 512],
                                    in_=fstage[:, i * 512:(i + 1) * 512]), dO, 16)
                        blk.sync(f_fdma)

                    for i, bi in enumerate(blks):
                        def f_vs(vector, i=i, bi=bi, mm_done=mm_done, sp=P.val(sS)):
                            vector.wait_ge(sT, mm_done)
                            if sp:
                                vector.wait_ge(sS, sp)
                            P.inc(vector.tensor_scalar(
                                gb[:], ps5[i][:], self.r2_all[:, bi:bi + 1],
                                None, ALU.mult), sV, 1)
                        blk.vector(f_vs)
                        evP[i] = P.val(sV)
                        vs_ok = P.val(sV)

                        def f_silu(scalar, vs_ok=vs_ok):
                            scalar.wait_ge(sV, vs_ok)
                            P.inc(scalar.activation(gb[:], gb[:], AF.Silu), sS, 1)
                        blk.scalar(f_silu)
                        si_ok = P.val(sS)

                        def f_sg(vector, i=i, ec=ec, si_ok=si_ok, sl_ok=sl_ok):
                            vector.wait_ge(sS, si_ok)
                            vector.wait_ge(dA, sl_ok)
                            P.inc(vector.tensor_tensor(
                                gb[:], gb[:],
                                stl[:, i * 2048 + ec * 512: i * 2048 + ec * 512 + 512],
                                ALU.mult), sV, 1)
                        blk.vector(f_sg)

                        def f_add(vector, i=i, par=par, x_ok=x_ok, g=outPrev):
                            vector.wait_ge(dA, x_ok)
                            if g:
                                vector.wait_ge(dO, g)
                            P.inc(vector.tensor_tensor(
                                ob[:, i * 512:(i + 1) * 512], gb[:],
                                xr[par][:, i * 512:(i + 1) * 512],
                                ALU.add), sV, 1)
                        blk.vector(f_add)
                    add_ok = P.val(sV)
                    xrW[par] = add_ok

                    def f_st(sync, ec=ec, blks=blks, add_ok=add_ok):
                        sync.wait_ge(sV, add_ok)
                        for i, bi in enumerate(blks):
                            P.inc(sync.dma_start(
                                out=self.out[bi * 128:(bi + 1) * 128, ec * 512:(ec + 1) * 512],
                                in_=ob[:, i * 512:(i + 1) * 512]), dO, 16)
                    blk.sync(f_st)
                    outPrev = P.val(dO)

    def finalize(self):
        P, blk, sm = self, self.blk, self.sm
        vals = [(sm[k], P.val(sm[k])) for k in ("dO", "dS", "dA", "sV", "sS", "sT", "sG", "sC")]

        def f_fin(sync):
            for s_, v_ in vals:
                if v_:
                    sync.wait_ge(s_, v_)
        blk.sync(f_fin)



_CACHE = {}


def _host_tables():
    import jax
    import jax.numpy as jnp
    with jax.default_device(jax.devices("cpu")[0]):
        inv_freq = 1.0 / ROPE_BASE ** (jnp.arange(0, D, 2, dtype=jnp.float32) / D)
        t = jnp.arange(2048, dtype=jnp.float32)[:, None] * inv_freq[None, :]
        return np.asarray(jnp.cos(t)), np.asarray(jnp.sin(t))


def _core_meta(p):
    own_blocks = [2 * j + p for j in range(8)]
    extra_blocks = [2 * j + (1 - p) for j in range(8)]
    own_rows = np.concatenate([np.arange(b * 128, (b + 1) * 128) for b in own_blocks])
    extra_rows = np.concatenate([np.arange(b * 128, (b + 1) * 128) for b in extra_blocks])
    return own_blocks, extra_blocks, own_rows, extra_rows


def _masks_for(p):
    own_blocks, extra_blocks, own_rows, extra_rows = _core_meta(p)
    m = np.zeros((2, 16, 128, 512), dtype=np.float32)
    for tau in (0, 1):
        qpos = own_rows[tau * 512:(tau + 1) * 512]
        for s in range(16):
            kb = own_blocks[s] if s < 8 else extra_blocks[s - 8]
            kpos = np.arange(kb * 128, (kb + 1) * 128)
            m[tau, s] = (qpos[None, :] >= kpos[:, None]).astype(np.float32)
    return np.ascontiguousarray(m.astype(ml_dtypes.bfloat16))


def _prep_inputs(inputs):
    x = np.asarray(inputs["L0_attn_x"])
    w_qkv = np.ascontiguousarray(np.asarray(inputs["L0_attn_w_qkv"]))
    w_out = np.ascontiguousarray(np.asarray(inputs["L0_attn_w_out"]))
    ln_g = np.ascontiguousarray(np.asarray(inputs["L0_attn_ln_scale"]))
    ln_b = np.ascontiguousarray(np.asarray(inputs["L0_attn_ln_bias"]))
    w13T = np.ascontiguousarray(np.asarray(inputs["L0_ffn_w13"]).T)
    w2T = np.ascontiguousarray(np.asarray(inputs["L0_ffn_w2"]).T)
    norm_w = np.ascontiguousarray(np.asarray(inputs["L0_ffn_norm_w"]))
    cos_tab, sin_tab = _host_tables()
    eye_bf = np.eye(128, dtype=ml_dtypes.bfloat16)
    eye_f32 = np.eye(128, dtype=np.float32)
    eh = np.zeros((128, 256), dtype=np.float32)
    for h in range(16):
        eh[:, h * 16 + h] = 1.0
    eh16 = np.ascontiguousarray(eh.astype(ml_dtypes.bfloat16))

    in_maps = []
    metas = []
    for c in range(8):
        b, p = c // 2, c % 2
        own_blocks, extra_blocks, own_rows, extra_rows = _core_meta(p)
        im = dict(
            x_own=np.ascontiguousarray(x[b][own_rows]),
            x_extra=np.ascontiguousarray(x[b][extra_rows]),
            w_qkv=w_qkv, w_out=w_out, w13T=w13T, w2T=w2T,
            ln_g=ln_g, ln_b=ln_b, norm_w=norm_w,
            cos_tab=np.ascontiguousarray(
                np.concatenate([cos_tab[own_rows], cos_tab[extra_rows]])
                .reshape(16, 128, 64).transpose(1, 0, 2).reshape(128, 16 * 64)),
            sin_tab=np.ascontiguousarray(
                np.concatenate([sin_tab[own_rows], sin_tab[extra_rows]])
                .reshape(16, 128, 64).transpose(1, 0, 2).reshape(128, 16 * 64)),
            masks=_masks_for(p),
            eye_bf=eye_bf, eye_f32=eye_f32, eh16=eh16,
        )
        in_maps.append(im)
        metas.append((b, own_rows))
    return in_maps, metas


def kernel(**inputs):
    if "nc" not in _CACHE:
        _CACHE["nc"] = Builder().build()
    nc = _CACHE["nc"]
    in_maps, metas = _prep_inputs(inputs)
    res = run_bass_kernel_spmd(nc, in_maps, core_ids=list(range(8)))
    out = np.zeros((B, N, E), dtype=np.float32)
    for c in range(8):
        b, own_rows = metas[c]
        out[b][own_rows] = np.asarray(res.results[c]["out"])
    return out
